# version 13
# baseline (speedup 1.0000x reference)
import numpy as np
import ml_dtypes
import concourse.bass as bass
import concourse.mybir as mybir
from concourse.bass_utils import run_bass_kernel_spmd
from contextlib import ExitStack

F32 = mybir.dt.float32
BF16 = mybir.dt.bfloat16
I32 = mybir.dt.int32
I8 = mybir.dt.int8
U8 = mybir.dt.uint8
AF = mybir.ActivationFunctionType
ALU = mybir.AluOpType
AX = mybir.AxisListType

DM = 4096
DFF = 11008
NCORE = 8
TLOC = 2048
TQ = 1024
NQH = 32
NKVH = 8
HD = 128
WIN = 1024
SM = HD ** -0.5
NEG = -1.0e30
NDMC = DM // 128
NFFC = DFF // 128

ENGS = ("tensor", "vector", "scalar", "sync", "gpsimd")
SERIALIZE_TP = False


class Sem:
    __slots__ = ("h", "n", "name")

    def __init__(self, h, name):
        self.h = h
        self.n = 0
        self.name = name


class Prog:
    def __init__(self, nc, stack):
        self.nc = nc
        self.ops = {e: [] for e in ENGS}
        self.sems = {}
        for s in ("pe", "dv", "ac", "qs", "qg", "gp"):
            self.sems[s] = Sem(stack.enter_context(nc.semaphore(s)), s)
        self._guard = None
        self._guarded = set()

    def phase(self):
        self._guard = {k: s.n for k, s in self.sems.items() if s.n > 0}
        self._guarded = set()

    def op(self, e, f, waits=(), inc=None, inc_n=1):
        wl = [(s.h, v) for (s, v) in waits if v and v > 0]
        if self._guard is not None and e not in self._guarded:
            self._guarded.add(e)
            wl = [(self.sems[k].h, v) for k, v in self._guard.items()] + wl
        tok = None
        ih = None
        if inc is not None:
            inc.n += inc_n
            tok = inc.n
            ih = inc.h

        def run(eng, f=f, wl=wl, ih=ih, inc_n=inc_n):
            for h, v in wl:
                eng.wait_ge(h, v)
            ins = f(eng)
            if ih is not None:
                ins.then_inc(ih, inc_n)
        self.ops[e].append(run)
        return tok

    def dma(self, e, dst, src, waits=(), transpose=False):
        q = self.sems["qs" if e == "sync" else "qg"]
        if transpose:
            w = list(waits)
            if SERIALIZE_TP and getattr(self, "_last_tp", 0):
                w.append((self.sems["qs"], self._last_tp))
            tok = self.op(e, lambda eng: eng.dma_start(dst, src, transpose=True),
                          waits=w, inc=q, inc_n=16)
            self._last_tp = tok
            return tok
        return self.op(e, lambda eng: eng.dma_start(dst, src), waits=waits,
                       inc=q, inc_n=16)

    def waitop(self, e, waits):
        wl = [(s.h, v) for (s, v) in waits if v and v > 0]

        def run(eng, wl=wl):
            for h, v in wl:
                eng.wait_ge(h, v)
        self.ops[e].append(run)

    def finalize(self):
        with self.nc.Block() as blk:
            for e in ENGS:
                def mk(e):
                    def body(eng):
                        for f in self.ops[e]:
                            f(eng)
                    return body
                getattr(blk, e)(mk(e))


def build_program(taps=()):
    taps = set(taps)
    nc = bass.Bass("TRN2", target_bir_lowering=False)

    x_in = nc.dram_tensor("x", [TLOC, DM], F32, kind="ExternalInput")
    wk = nc.dram_tensor("wk", [DM, NKVH * HD], F32, kind="ExternalInput")
    wv = nc.dram_tensor("wv", [DM, NKVH * HD], F32, kind="ExternalInput")
    D1B_ = 128 * 16 * 512
    gt = nc.dram_tensor("gt", [1, 44 * D1B_], I8, kind="ExternalInput")
    ut = nc.dram_tensor("ut", [1, 44 * D1B_], I8, kind="ExternalInput")
    dwt = nc.dram_tensor("dwt", [1, 48 * D1B_], I8, kind="ExternalInput")
    wqd = nc.dram_tensor("wqd", [1, DM * NQH * HD], F32, kind="ExternalInput")
    wod = nc.dram_tensor("wod", [1, NQH * HD * DM], F32, kind="ExternalInput")
    gs_in = nc.dram_tensor("gs", [128, NFFC], F32, kind="ExternalInput")
    us_in = nc.dram_tensor("us", [128, NFFC], F32, kind="ExternalInput")
    ds_in = nc.dram_tensor("ds", [128, NDMC], F32, kind="ExternalInput")
    cs_in = nc.dram_tensor("cs", [128, TLOC], F32, kind="ExternalInput")
    mk_in = nc.dram_tensor("mask", [8, 128, 1152], BF16, kind="ExternalInput")
    out = nc.dram_tensor("out", [TQ, DM], F32, kind="ExternalOutput")

    xnscr = nc.dram_tensor(
        "xnscr", [128, 16 * DM], BF16,
        kind="ExternalOutput" if "xn" in taps else "Internal")
    o1scr = nc.dram_tensor(
        "o1scr", [2, 128, NDMC, 512], BF16,
        kind="ExternalOutput" if "o1" in taps else "Internal")
    hscr = nc.dram_tensor(
        "hscr", [128, NFFC * 1024], BF16,
        kind="ExternalOutput" if "h" in taps else "Internal")
    qscr = nc.dram_tensor(
        "qscr", [128, NQH * 1024], BF16,
        kind="ExternalOutput" if "q" in taps else "Internal")

    def mktap(name, shape, dtype=BF16):
        return (nc.dram_tensor(name, shape, dtype, kind="ExternalOutput")
                if name in taps else None)

    k_tap = mktap("k_tap", [128, NKVH * TLOC])
    v_tap = mktap("v_tap", [128, 16 * NKVH * HD])
    q_tap = mktap("q_tap", [2, 128, NQH * 512])
    cat_tap = mktap("cat_tap", [2, 128, NQH * 512])
    h_tap = mktap("h_tap", [2, 128, NFFC * 512])

    stack = ExitStack()
    A = nc.alloc_sbuf_tensor

    cs_sb = A("cs_sb", [128, TLOC], F32)
    gs_sb = A("gs_sb", [128, NFFC], F32)
    us_sb = A("us_sb", [128, NFFC], F32)
    ds_sb = A("ds_sb", [128, NDMC], F32)
    epsb = A("epsb", [128, 1], F32)
    smv = [A(f"smv{i}", [128, 1], F32) for i in range(2)]
    ssqv = [A(f"ssqv{i}", [128, 1], F32) for i in range(2)]
    meanv = [A(f"meanv{i}", [128, 1], F32) for i in range(2)]
    varv = [A(f"varv{i}", [128, 1], F32) for i in range(2)]
    stdv = [A(f"stdv{i}", [128, 1], F32) for i in range(2)]
    rstd = [A(f"rstd{i}", [128, 1], F32) for i in range(2)]
    nmrv = [A(f"nmrv{i}", [128, 1], F32) for i in range(2)]
    rt1 = A("rt1", [64, 512], F32)
    rt2 = A("rt2", [64, 512], F32)
    nmx = [A(f"nmx{i}", [128, 1], F32) for i in range(2)]
    sbias = [A(f"sbias{i}", [128, 1], F32) for i in range(2)]
    sums = [A(f"sums{i}", [128, 1], F32) for i in range(2)]
    rsum = [A(f"rsum{i}", [128, 1], F32) for i in range(2)]

    ident_f = nc.inline_tensor(np.eye(128, dtype=np.float32), name="identf")
    idf = A("idf", [128, 128], F32)

    ARENA = 188416
    arena = A("arena", [128, ARENA], U8)
    base = nc.lookup_mloc(arena).addr
    assert nc.sbuf_bytes_remaining >= 0

    def at(name, shape, dtype, off):
        nbytes = 1
        for s_ in shape[1:]:
            nbytes *= s_
        nbytes *= mybir.dt.size(dtype)
        assert off + nbytes <= ARENA, f"{name}: {off}+{nbytes} > {ARENA}"
        return nc.alloc_sbuf_tensor_at(name, shape, dtype, offset=base + off)

    xb0 = [at(f"xb0_{i}", [128, DM], F32, i * 16384) for i in range(2)]
    xt0 = [at(f"xt0_{i}", [128, DM], BF16, 32768 + i * 8192) for i in range(2)]
    xf0 = [at(f"xf0_{i}", [128, DM], BF16, 49152 + i * 8192) for i in range(2)]
    K_fm = at("K_fm", [128, NKVH * TLOC], BF16, 0)
    V_tm = at("V_tm", [128, 16 * NKVH * HD], BF16, 32768)
    PB = 65536
    wkb = at("wkb", [128, NDMC * 512], BF16, PB)
    wvb = at("wvb", [128, NDMC * 512], BF16, PB + 32768)
    xsl = [at(f"xsl{i}", [128, 2 * DM], BF16, PB + 65536 + i * 16384)
           for i in range(2)]
    xqFq = at("xqFq", [128, 8 * DM], BF16, PB)
    qwb = [at(f"qwb{i}", [128, 4 * 256], BF16, PB + 65536 + i * 2048)
           for i in range(4)]
    Qst = [at(f"Qst{i}", [128, 2 * 1024], BF16, PB + 73728 + i * 4096)
           for i in range(2)]
    Qh = at("Qh", [128, NQH * 512], BF16, PB + 32768)
    cat = at("cat", [128, NQH * 512], BF16, PB)
    mtile = [at(f"mtile{i}", [128, 1152], BF16, PB + 65536 + i * 2304)
             for i in range(2)]
    sc_sb = [at(f"sc_sb{i}", [128, 1152], F32, PB + 70144 + i * 4608)
             for i in range(2)]
    sc0_sb = [at(f"sc0_sb{i}", [128, 1152], F32, PB + 89088 + i * 4608)
              for i in range(2)]
    P_sb = [at(f"P_sb{i}", [128, 1152], BF16, PB + 79360 + i * 2304)
            for i in range(2)]
    PT_sb = [at(f"PT_sb{i}", [128, 1152], BF16, PB + 83968 + i * 2304)
             for i in range(2)]
    av_sb = [at(f"av_sb{i}", [128, 128], BF16, PB + 88576 + i * 256)
             for i in range(2)]
    wob = [at(f"wob{i}", [128, 4 * 512], BF16, PB + 103424 + i * 4096)
           for i in range(4)]
    o1b = at("o1b", [128, 8 * 512], BF16, PB + 98304)
    xqF = at("xqF", [128, 8 * DM], BF16, 0)
    gtb = [at(f"gtb{i}", [128, 16 * 512], BF16, 65536 + i * 16384)
           for i in range(3)]
    utb = [at(f"utb{i}", [128, 16 * 512], BF16, 114688 + i * 16384)
           for i in range(3)]
    sgb = [at(f"sgb{i}", [128, 1024], BF16, 163840 + i * 2048) for i in range(2)]
    sub = [at(f"sub{i}", [128, 1024], BF16, 167936 + i * 2048) for i in range(2)]
    hst = [at(f"hst{i}", [128, 1024], BF16, 172032 + i * 2048) for i in range(2)]
    h_sb = at("h_sb", [128, NFFC * 512], BF16, 0)
    dtb = [at(f"dtb{i}", [128, 16 * 512], BF16, 90112 + i * 16384)
           for i in range(2)]
    dtb.append(at("dtb2", [128, 16 * 512], BF16, 167936))
    ostg = at("ostg", [128, DM], F32, 0)
    o1t = [at(f"o1t{i}", [128, 4 * 512], BF16, 122880 + i * 4096)
           for i in range(2)]
    res_f = [at(f"res_f{i}", [128, 512], F32, 131072 + i * 2048)
             for i in range(2)]
    outb = at("outb", [128, 4 * DM], BF16, 135168)

    PS = nc.alloc_psum_tensor("PS", [128, 4096], F32)

    def bankf(i, w=512):
        return PS[:, i * 512: i * 512 + w]

    p = Prog(nc, stack)
    pe, dv, ac, qs, qg, gp = (p.sems[k] for k in ("pe", "dv", "ac", "qs", "qg", "gp"))

    p.dma("sync", cs_sb[:], cs_in.ap())
    p.dma("sync", gs_sb[:], gs_in.ap())
    p.dma("sync", us_sb[:], us_in.ap())
    p.dma("sync", ds_sb[:], ds_in.ap())
    p.dma("sync", idf[:], ident_f.ap())
    pre_qs = qs.n
    p.op("vector", lambda v: v.memset(epsb[:], 1e-5),
         waits=[(qs, pre_qs)], inc=dv)

    xnscr3 = xnscr.ap().rearrange("p (tt d) -> p tt d", d=DM)

    p.phase()
    x2 = x_in.ap().rearrange("(tt p) d -> tt p d", p=128)
    ld_tok = [0] * 17
    ssq_tok = [0] * 17
    red_tok = [0] * 17
    xt_tok = [0] * 17
    dma_xn = [0] * 17
    ev_last = [0] * 17
    tp_last = [0] * 17
    tpb_ev = [0, 0]

    def p0_front(tt, pv=None):
        b = tt % 2
        w = []
        if pv is not None:
            w += [(dv, xt_tok[pv]), (ac, ssq_tok[pv])]
        ld_tok[tt] = p.dma("sync", xb0[b][:], x2[tt], waits=w)
        w = [(qs, ld_tok[tt])]
        if pv is not None:
            w.append((qs, ev_last[pv]))
        ssq_tok[tt] = p.op("scalar", lambda s, b=b: s.activation(
            xt0[b][:], xb0[b][:], AF.Square, accum_out=ssqv[b][:]),
            waits=w, inc=ac)
        red_tok[tt] = p.op("vector", lambda v, b=b: v.reduce_sum(
            smv[b][:], xb0[b][:], axis=AX.X),
            waits=[(qs, ld_tok[tt])], inc=dv)

    def p0_back(tt, pv=None):
        b = tt % 2
        t = p.op("vector", lambda v, b=b: v.tensor_scalar_mul(
            meanv[b][:], smv[b][:], 1.0 / DM), waits=[(dv, red_tok[tt])], inc=dv)
        t = p.op("vector", lambda v, b=b: v.tensor_scalar_mul(
            varv[b][:], ssqv[b][:], 1.0 / DM),
            waits=[(ac, ssq_tok[tt]), (dv, t)], inc=dv)
        t = p.op("vector", lambda v, b=b: v.tensor_mul(
            stdv[b][:], meanv[b][:], meanv[b][:]), waits=[(dv, t)], inc=dv)
        t = p.op("vector", lambda v, b=b: v.tensor_sub(
            varv[b][:], varv[b][:], stdv[b][:]), waits=[(dv, t)], inc=dv)
        t_std = p.op("scalar", lambda s, b=b: s.activation(
            stdv[b][:], varv[b][:], AF.Sqrt, bias=epsb[:]),
            waits=[(dv, t)], inc=ac)
        t = p.op("vector", lambda v, b=b: v.reciprocal(
            rstd[b][:], stdv[b][:]), waits=[(ac, t_std)], inc=dv)
        t = p.op("vector", lambda v, b=b: v.scalar_tensor_tensor(
            nmrv[b][:], meanv[b][:], -1.0, rstd[b][:],
            op0=ALU.mult, op1=ALU.mult), waits=[(dv, t)], inc=dv)
        w = [(dv, t), (ac, ssq_tok[tt])]
        if pv is not None:
            w.append((qs, ev_last[pv]))
        xt_tok[tt] = p.op("vector", lambda v, b=b: v.tensor_scalar(
            xt0[b][:], xb0[b][:], rstd[b][:], nmrv[b][:],
            op0=ALU.mult, op1=ALU.add), waits=w, inc=dv)
        w = [(dv, xt_tok[tt])]
        if pv is not None:
            w.append((qs, dma_xn[pv]))
        ev_last[tt] = p.dma(
            "sync",
            xf0[b].ap().rearrange("p (c t) -> p c t", t=128),
            xt0[b].ap().rearrange("p (c t) -> p c t", t=128),
            waits=w, transpose=True)
        tp_last[tt] = 0
        dma_xn[tt] = p.dma("sync", xnscr3[:, tt, :], xf0[b][:],
                           waits=[(qs, ev_last[tt])])

    P0ORD = list(range(8, 16)) + list(range(0, 8))
    for i_, tt in enumerate(P0ORD):
        p0_front(tt, P0ORD[i_ - 2] if i_ >= 2 else None)
        if i_ >= 1:
            p0_back(P0ORD[i_ - 1], P0ORD[i_ - 3] if i_ >= 3 else None)
    p0_back(P0ORD[15], P0ORD[13])

    qtiles_done = dma_xn[15]
    WQB = 128 * 4 * 256
    xqF_tok0 = p.dma("sync", xqFq[:], xnscr.ap()[:, 8 * DM:16 * DM],
                     waits=[(qs, qtiles_done)])
    xqF3g = xqFq.ap().rearrange("p (tt d) -> p tt d", d=DM)
    qw_used = [0, 0, 0, 0]
    qrope = [0] * 16
    qst_dma = [0] * 16
    qmm = [0] * 16

    def qf_mms(ag):
        par = ag % 2
        for dq in range(8):
            wb = (ag * 8 + dq) % 4
            w = [(pe, qw_used[wb])] if qw_used[wb] else []
            off = (ag * 8 + dq) * WQB
            wq_tok = p.dma("gpsimd", qwb[wb][:],
                           wqd.ap()[0, off: off + WQB]
                           .rearrange("(p k) -> p k", p=128), waits=w)
            for cl in range(4):
                dm = dq * 4 + cl
                for ach in range(2):
                    for th in range(2):
                        wts = []
                        if cl == 0 and ach == 0 and th == 0:
                            wts = [(qg, wq_tok)]
                            if dq == 0:
                                wts.append((qs, xqF_tok0))
                                if ag >= 2:
                                    wts.append((dv, qrope[ag - 2]))
                        bk = ach * 2 + th + 4 * par
                        p.op("tensor", lambda t_, wb=wb, cl=cl, ach=ach, th=th, dm=dm, bk=bk: t_.matmul(
                            bankf(bk),
                            qwb[wb][:, cl * 256 + ach * 128: cl * 256 + ach * 128 + 128],
                            xqF3g[:, th * 4:(th + 1) * 4, dm * 128:(dm + 1) * 128],
                            start=(dm == 0), stop=(dm == NDMC - 1)),
                            waits=wts,
                            inc=pe if (cl == 3 and ach == 1 and th == 1) else None)
            qw_used[wb] = pe.n
        qmm[ag] = pe.n

    def qf_ev(ag):
        par = ag % 2
        first = True
        for ach in range(2):
            for th in range(2):
                bk = ach * 2 + th + 4 * par
                pk = bankf(bk)
                cosw = cs_sb[0:64, 1024 + th * 512: 1024 + (th + 1) * 512]
                sinw = cs_sb[64:128, 1024 + th * 512: 1024 + (th + 1) * 512]
                d0 = Qst[par][0:64, ach * 1024 + th * 512: ach * 1024 + (th + 1) * 512]
                d1 = Qst[par][64:128, ach * 1024 + th * 512: ach * 1024 + (th + 1) * 512]
                w0 = []
                if first:
                    w0 = [(pe, qmm[ag])]
                    if ag >= 2:
                        w0.append((qs, qst_dma[ag - 2]))
                    first = False
                p.op("vector", lambda v, pk=pk, c=cosw: v.tensor_mul(rt1[:], pk[0:64, :], c), waits=w0)
                p.op("vector", lambda v, pk=pk, s=sinw: v.tensor_mul(rt2[:], pk[64:128, :], s))
                p.op("vector", lambda v, d=d0: v.tensor_sub(d, rt1[:], rt2[:]))
                p.op("vector", lambda v, pk=pk, s=sinw: v.tensor_mul(rt1[:], pk[0:64, :], s))
                p.op("vector", lambda v, pk=pk, c=cosw: v.tensor_mul(rt2[:], pk[64:128, :], c))
                qrope[ag] = p.op("vector", lambda v, d=d1: v.tensor_add(d, rt1[:], rt2[:]), inc=dv)
        qst_dma[ag] = p.dma("sync", qscr.ap()[:, ag * 2048:(ag + 1) * 2048],
                            Qst[par][:], waits=[(dv, qrope[ag])])

    for ag in range(16):
        qf_mms(ag)
        if ag >= 1:
            qf_ev(ag - 1)
    qf_ev(15)

    p.phase()
    wk3 = wk.ap().rearrange("(c p) f -> p c f", p=128)
    wv3 = wv.ap().rearrange("(c p) f -> p c f", p=128)
    xsl3 = [xsl[i].ap().rearrange("p (tt d) -> p tt d", d=DM) for i in range(2)]
    NKVC = 16
    kev = [0] * NKVC
    vev = [0] * NKVC
    mmt = [0] * NKVC
    xs_tok = [0] * NKVC
    wvb_toks = [0, 0]

    def kv_mm(it):
        fh, tch = it // 8, it % 8
        b = it % 2
        par = it % 2
        if tch == 0:
            w = [(pe, mmt[it - 1])] if it >= 1 else []
            p.dma("gpsimd", wkb[:], wk3[:, :, fh * 512:(fh + 1) * 512], waits=w)
            wvb_toks[fh] = p.dma("gpsimd", wvb[:],
                                 wv3[:, :, fh * 512:(fh + 1) * 512])
        w = [(pe, mmt[it - 2])] if it >= 2 else []
        xs_tok[it] = p.dma("sync", xsl[b][:],
                           xnscr.ap()[:, tch * 2 * DM:(tch + 1) * 2 * DM],
                           waits=w)
        for dm in range(NDMC):
            wts = []
            if dm == 0:
                wts = [(qs, xs_tok[it]), (qg, wvb_toks[fh])]
                if it >= 2:
                    wts += [(dv, kev[it - 2]), (ac, vev[it - 2])]
            for fc in range(4):
                bk = (fc // 2) + 4 * par
                sof = (fc % 2) * 256
                p.op("tensor", lambda t_, dm=dm, fc=fc, bk=bk, sof=sof, b=b: t_.matmul(
                    PS[:, bk * 512 + sof: bk * 512 + sof + 256],
                    wkb[:, dm * 512 + fc * 128: dm * 512 + fc * 128 + 128],
                    xsl3[b][:, :, dm * 128:(dm + 1) * 128],
                    start=(dm == 0), stop=(dm == NDMC - 1)),
                    waits=wts if fc == 0 else [])
            for ttl in range(2):
                last = (dm == NDMC - 1 and ttl == 1)
                p.op("tensor", lambda t_, dm=dm, ttl=ttl, par=par, b=b: t_.matmul(
                    bankf(2 + ttl + 4 * par),
                    xsl3[b][:, ttl, dm * 128:(dm + 1) * 128],
                    wvb[:, dm * 512:(dm + 1) * 512],
                    start=(dm == 0), stop=(dm == NDMC - 1)),
                    inc=pe if last else None)
        mmt[it] = pe.n

    def kv_ev(it):
        fh, tch = it // 8, it % 8
        par = it % 2
        for fc in range(4):
            kh = fh * 4 + fc
            bk = (fc // 2) + 4 * par
            sof = (fc % 2) * 256
            pk = PS[:, bk * 512 + sof: bk * 512 + sof + 256]
            cosw = cs_sb[0:64, tch * 256:(tch + 1) * 256]
            sinw = cs_sb[64:128, tch * 256:(tch + 1) * 256]
            d0 = K_fm[0:64, kh * TLOC + tch * 256: kh * TLOC + (tch + 1) * 256]
            d1 = K_fm[64:128, kh * TLOC + tch * 256: kh * TLOC + (tch + 1) * 256]
            w0 = [(pe, mmt[it])] if fc == 0 else []
            p.op("vector", lambda v, pk=pk, c=cosw: v.tensor_mul(rt1[:, 0:256], pk[0:64, :], c), waits=w0)
            p.op("vector", lambda v, pk=pk, s=sinw: v.tensor_mul(rt2[:, 0:256], pk[64:128, :], s))
            p.op("vector", lambda v, d=d0: v.tensor_sub(d, rt1[:, 0:256], rt2[:, 0:256]))
            p.op("vector", lambda v, pk=pk, s=sinw: v.tensor_mul(rt1[:, 0:256], pk[0:64, :], s))
            p.op("vector", lambda v, pk=pk, c=cosw: v.tensor_mul(rt2[:, 0:256], pk[64:128, :], c))
            kev[it] = p.op("vector", lambda v, d=d1: v.tensor_add(d, rt1[:, 0:256], rt2[:, 0:256]), inc=dv)
        for ttl in range(2):
            tc = tch * 2 + ttl
            fh_ = fh
            vev[it] = p.op("scalar", lambda s, tc=tc, ttl=ttl, par=par, fh=fh_: s.copy(
                V_tm[:, tc * 1024 + fh * 512: tc * 1024 + (fh + 1) * 512],
                bankf(2 + ttl + 4 * par)),
                waits=[(pe, mmt[it])] if ttl == 0 else [], inc=ac)

    for it in range(NKVC):
        kv_mm(it)
        if it >= 1:
            kv_ev(it - 1)
    kv_ev(NKVC - 1)
    if k_tap is not None:
        p.dma("sync", k_tap.ap(), K_fm[:], waits=[(dv, kev[NKVC - 1])])
    if v_tap is not None:
        p.dma("sync", v_tap.ap(), V_tm[:], waits=[(ac, vev[NKVC - 1])])

    o1_dma_tok = 0
    p3b_lastA = 0
    qscr3 = qscr.ap().rearrange("p (hd t) -> p hd t", t=1024)
    for h in range(2):
        p.phase()
        qh_tok = p.dma("sync", Qh[:], qscr3[:, :, h * 512:(h + 1) * 512],
                       waits=[(pe, p3b_lastA)] if p3b_lastA else [])

        p.phase()
        S = [PS[:, 0:1152], PS[:, 1536:2688]]
        Ssl = [(PS[:, 0:512], PS[:, 512:1024], PS[:, 1024:1152]),
               (PS[:, 1536:2048], PS[:, 2048:2560], PS[:, 2560:2688])]
        O_ps = [PS[:, 3072:3200], PS[:, 3584:3712]]
        mk3 = mk_in.ap()
        NH = 128
        m_tok = [0, 0]
        m_read = [0, 0]

        def hqt(i):
            return i // 32, i % 32

        tokA = [0] * NH
        tokE0 = [0] * NH
        tokB = [0] * NH
        tokB2 = [0] * NH
        tokC = [0] * NH
        tokG = [0] * NH
        tokT = [0] * NH
        tokF = [0] * NH
        tokN = [0] * NH
        tokH = [0] * NH

        def stA(i):
            qt, hd_ = hqt(i)
            qg_ = h * 4 + qt
            par = i % 2
            mb = qt % 2
            kh = hd_ // 4
            if hd_ == 0:
                w = [(qg, m_tok[mb])] if m_tok[mb] else []
                w += [(dv, m_read[mb])] if m_read[mb] else []
                m_tok[mb] = p.dma("gpsimd", mtile[mb][:], mk3[qg_], waits=w)
            wts = [(qg, m_tok[mb])] if hd_ == 0 else []
            if i == 0:
                wts.append((qs, qh_tok))
            if i >= 2:
                wts.append((dv, tokB[i - 2]))
            for j, wdt in enumerate((512, 512, 128)):
                off = qg_ * 128 + j * 512
                p.op("tensor", lambda t_, par=par, j=j, hd_=hd_, qt=qt, kh=kh, off=off, wdt=wdt: t_.matmul(
                    Ssl[par][j],
                    Qh[:, hd_ * 512 + qt * 128: hd_ * 512 + qt * 128 + 128],
                    K_fm[:, kh * TLOC + off: kh * TLOC + off + wdt],
                    start=True, stop=True),
                    waits=wts if j == 0 else [],
                    inc=pe if j == 2 else None)
            tokA[i] = pe.n

        def stBC(i):
            qt, hd_ = hqt(i)
            par = i % 2
            mb = qt % 2
            w = [(pe, tokA[i]), (qg, m_tok[mb])]
            if i >= 2:
                w.append((ac, tokC[i - 2]))
            tokB[i] = p.op("vector", lambda v, par=par, mb=mb: v.tensor_add(
                sc_sb[par][:], S[par], mtile[mb][:]), waits=w, inc=dv)
            m_read[mb] = tokB[i]
            t = p.op("vector", lambda v, par=par: v.reduce_max(
                nmx[par][:], sc_sb[par][:], axis=AX.X, negate=True),
                waits=[], inc=dv)
            w = [(dv, t)]
            if i >= 2:
                w.append((ac, tokC[i - 2]))
            tokB2[i] = p.op("vector", lambda v, par=par: v.tensor_scalar_mul(
                sbias[par][:], nmx[par][:], SM), waits=w, inc=dv)
            w = [(dv, tokB2[i])]
            if i >= 2:
                w += [(qs, tokT[i - 2]), (dv, tokG[i - 2])]
            tokC[i] = p.op("scalar", lambda s, par=par: s.activation(
                P_sb[par][:], sc_sb[par][:], AF.Exp, bias=sbias[par][:], scale=SM,
                accum_out=sums[par][:]), waits=w, inc=ac)
            tokG[i] = p.op("vector", lambda v, par=par: v.reciprocal(
                rsum[par][:], sums[par][:]), waits=[(ac, tokC[i])], inc=dv)
            w = [(ac, tokC[i])]
            if i >= 2:
                w.append((pe, tokF[i - 2]))
            tokT[i] = p.dma(
                "sync",
                PT_sb[par].ap().rearrange("p (c t) -> p c t", t=128),
                P_sb[par].ap().rearrange("p (c t) -> p c t", t=128),
                waits=w, transpose=True)

        def stFN(i):
            qt, hd_ = hqt(i)
            qg_ = h * 4 + qt
            par = i % 2
            kh = hd_ // 4
            for j in range(9):
                w = []
                if j == 0:
                    w = [(qs, tokT[i])]
                    if i >= 2:
                        w.append((ac, tokN[i - 2]))
                p.op("tensor", lambda t_, par=par, j=j, kh=kh, qg_=qg_: t_.matmul(
                    O_ps[par],
                    PT_sb[par][:, j * 128:(j + 1) * 128],
                    V_tm[:, (qg_ + j) * 1024 + kh * 128: (qg_ + j) * 1024 + kh * 128 + 128],
                    start=(j == 0), stop=(j == 8)),
                    waits=w, inc=pe if j == 8 else None)
            tokF[i] = pe.n
            w = [(pe, tokF[i]), (dv, tokG[i])]
            if i >= 2:
                w.append((qs, tokH[i - 2]))
            tokN[i] = p.op("scalar", lambda s, par=par: s.activation(
                av_sb[par][:], O_ps[par], AF.Copy, scale=rsum[par][:]),
                waits=w, inc=ac)
            tokH[i] = p.dma("sync",
                            cat[:, hd_ * 512 + qt * 128: hd_ * 512 + qt * 128 + 128],
                            av_sb[par][:], waits=[(ac, tokN[i])], transpose=True)

        for i in range(NH + 2):
            if i < NH:
                stA(i)
            if 1 <= i <= NH:
                stBC(i - 1)
            if i >= 2:
                stFN(i - 2)
        p3b_lastA = tokA[NH - 1]
        if cat_tap is not None:
            p.dma("sync", cat_tap.ap()[h], cat[:], waits=[(qs, tokH[NH - 1])])

        p.phase()
        o1scr_h = o1scr.ap()[h]
        o1b3 = o1b.ap().rearrange("p (dc t) -> p dc t", t=512)
        wo_used = [0, 0, 0, 0]
        wo_mm = [0] * 8
        wo_dma = [0] * 8

        def wo_mms(dg):
            par = dg % 2
            for aq in range(8):
                wb = (dg * 8 + aq) % 4
                w = [(pe, wo_used[wb])] if wo_used[wb] else []
                off = (dg * 8 + aq) * (128 * 4 * 512)
                wo_tok = p.dma("gpsimd", wob[wb][:],
                               wod.ap()[0, off: off + 128 * 4 * 512]
                               .rearrange("(p k) -> p k", p=128), waits=w)
                for cl in range(4):
                    a = aq * 4 + cl
                    for dc in range(4):
                        wts = []
                        if cl == 0 and dc == 0:
                            wts = [(qg, wo_tok)]
                            if aq == 0:
                                wts.append((qs, tokH[NH - 1]))
                                if dg >= 2:
                                    wts.append((qs, wo_dma[dg - 2]))
                                elif h > 0:
                                    wts.append((qs, o1_dma_tok))
                        last = (cl == 3 and dc == 3)
                        p.op("tensor", lambda t_, wb=wb, cl=cl, dc=dc, a=a, par=par: t_.matmul(
                            bankf(dc + 4 * par),
                            wob[wb][:, cl * 512 + dc * 128: cl * 512 + dc * 128 + 128],
                            cat[:, a * 512:(a + 1) * 512],
                            start=(a == 0), stop=(a == 31)),
                            waits=wts, inc=pe if last else None)
                wo_used[wb] = pe.n
            wo_mm[dg] = pe.n

        def wo_ev(dg):
            par = dg % 2
            ev = 0
            for dc in range(4):
                w = [(pe, wo_mm[dg])] if dc == 0 else []
                if dc == 0 and dg >= 2:
                    w.append((qs, wo_dma[dg - 2]))
                ev = p.op("scalar", lambda s, dc=dc, par=par: s.copy(
                    o1b3[:, 4 * par + dc, :], bankf(dc + 4 * par)),
                    waits=w, inc=ac if dc == 3 else None)
            wo_dma[dg] = p.dma("sync", o1scr_h[:, dg * 4:(dg + 1) * 4, :],
                               o1b3[:, 4 * par:4 * par + 4, :], waits=[(ac, ev)])

        for dg in range(8):
            wo_mms(dg)
            if dg >= 1:
                wo_ev(dg - 1)
        wo_ev(7)
        o1_dma_tok = wo_dma[7]
        p3c_cat_done = pe.n

    p.phase()
    xqF_tok = p.dma("sync", xqF[:], xnscr.ap()[:, 8 * DM:16 * DM])
    xqF3 = xqF.ap().rearrange("p (tt d) -> p tt d", d=DM)
    D1B = 128 * 16 * 512
    gtb_used = [0, 0, 0]
    utb_used = [0, 0, 0]
    d1_ut_tok = [0, 0]
    hst_dma = [0] * NFFC
    hm_tok = [0] * NFFC
    d1_mm = [0] * NFFC

    def d1_mms(fg):
        par = fg % 2
        fp = fg // 4
        fl4 = fg % 4
        for dh_ in range(2):
            wb = (fp * 2 + dh_) % 3
            if fl4 == 0:
                wide = min(4, NFFC - fg)
                nel = 128 * 16 * wide * 128
                off = (fp * 2 + dh_) * D1B
                w = [(pe, gtb_used[wb])] if gtb_used[wb] else []
                p.dma("gpsimd",
                      gtb[wb].ap().rearrange("p (a b) -> p a b", b=512)[:, :, 0:wide * 128],
                      gt.ap()[0, off: off + nel]
                      .rearrange("(p k) -> p k", p=128), waits=w)
                w = [(pe, utb_used[wb])] if utb_used[wb] else []
                d1_ut_tok[dh_] = p.dma(
                    "gpsimd",
                    utb[wb].ap().rearrange("p (a b) -> p a b", b=512)[:, :, 0:wide * 128],
                    ut.ap()[0, off: off + nel]
                    .rearrange("(p k) -> p k", p=128), waits=w)
            ut_tok = d1_ut_tok[dh_]
            for dml in range(16):
                dm = dh_ * 16 + dml
                wts = []
                if dml == 0:
                    wts = [(qg, ut_tok)]
                    if fg == 0 and dh_ == 0:
                        wts.append((qs, xqF_tok))
                    if dh_ == 0 and fg >= 2:
                        wts.append((dv, hm_tok[fg - 2]))
                for th in range(2):
                    p.op("tensor", lambda t_, wb=wb, dml=dml, dm=dm, th=th, par=par: t_.matmul(
                        bankf(th + 4 * par),
                        gtb[wb][:, dml * 512 + fl4 * 128: dml * 512 + fl4 * 128 + 128],
                        xqF3[:, th * 4:(th + 1) * 4, dm * 128:(dm + 1) * 128],
                        start=(dm == 0), stop=(dm == NDMC - 1)),
                        waits=wts if th == 0 else ())
                for th in range(2):
                    last = (dml == 15 and th == 1)
                    p.op("tensor", lambda t_, wb=wb, dml=dml, dm=dm, th=th, par=par: t_.matmul(
                        bankf(2 + th + 4 * par),
                        utb[wb][:, dml * 512 + fl4 * 128: dml * 512 + fl4 * 128 + 128],
                        xqF3[:, th * 4:(th + 1) * 4, dm * 128:(dm + 1) * 128],
                        start=(dm == 0), stop=(dm == NDMC - 1)),
                        inc=pe if last else None)
            if fl4 == 3 or fg == NFFC - 1:
                gtb_used[wb] = pe.n
                utb_used[wb] = pe.n
        d1_mm[fg] = pe.n

    def d1_ev(fg):
        par = fg % 2
        w0 = [(pe, d1_mm[fg])]
        if fg >= 2:
            w0.append((qs, hst_dma[fg - 2]))
        tmul = 0
        for th in range(2):
            t_sg = p.op("scalar", lambda s, th=th, par=par, fg=fg: s.activation(
                sgb[par][:, th * 512:(th + 1) * 512], bankf(th + 4 * par),
                AF.Silu, scale=gs_sb[:, fg:fg + 1]),
                waits=w0 if th == 0 else [], inc=ac)
            t_su = p.op("vector", lambda v, th=th, par=par, fg=fg: v.tensor_scalar_mul(
                sub[par][:, th * 512:(th + 1) * 512], bankf(2 + th + 4 * par),
                us_sb[:, fg:fg + 1]),
                waits=w0 if th == 0 else [], inc=dv)
            tmul = p.op("vector", lambda v, th=th, par=par: v.tensor_mul(
                hst[par][:, th * 512:(th + 1) * 512],
                sgb[par][:, th * 512:(th + 1) * 512],
                sub[par][:, th * 512:(th + 1) * 512]),
                waits=[(ac, t_sg), (dv, t_su)], inc=dv)
        hm_tok[fg] = tmul
        hst_dma[fg] = p.dma("sync", hscr.ap()[:, fg * 1024:(fg + 1) * 1024],
                            hst[par][:], waits=[(dv, tmul)])

    for fg in range(NFFC):
        d1_mms(fg)
        if fg >= 1:
            d1_ev(fg - 1)
    d1_ev(NFFC - 1)
    d1_last_mm = d1_mm[NFFC - 1]

    FOS = [8] * 10 + [6]
    d2_mm_last = 0
    dtb_used = [0, 0, 0]
    o1_used = [0, 0]
    res_used = [0, 0]
    stt_used = [0, 0]
    d2_bank_free = {0: hm_tok[NFFC - 2], 1: hm_tok[NFFC - 1]}
    out_store_tok = 0
    out_cp_last = 0
    for h in range(2):
        x2q = x_in.ap().rearrange("(tt p) d -> tt p d", p=128)
        outb3 = outb.ap().rearrange("p (tt d) -> p tt d", d=DM)
        xres_tok = 0
        for tt in range(4):
            w = []
            if tt == 0 and h == 0:
                w = [(pe, d1_last_mm), (qs, hst_dma[NFFC - 1])]
            if tt == 0 and h == 1:
                w = [(dv, out_cp_last)]
            xres_tok = p.dma("gpsimd", outb3[:, tt, :], x2q[8 + h * 4 + tt],
                             waits=w)
        o1scr_h = o1scr.ap()[h]
        h3 = h_sb.ap().rearrange("p (fc t) -> p fc t", t=512)
        hscr3 = hscr.ap().rearrange("p (fc t) -> p fc t", t=1024)
        hld = [0] * 11
        fo_base = 0
        for fo in range(11):
            fosz = FOS[fo]
            w = [(qs, hst_dma[fo_base + fosz - 1])]
            if h == 0 and fo == 0:
                w.append((pe, d1_last_mm))
            if h == 1 and fo == 0:
                w.append((pe, d2_mm_last))
                w.append((qs, out_store_tok))
            hld[fo] = p.dma("sync",
                            h3[:, fo * 8: fo * 8 + fosz, :],
                            hscr3[:, fo * 8: fo * 8 + fosz, h * 512:(h + 1) * 512],
                            waits=w)
            fo_base += fosz
        add_tok = [0] * 8
        d2_mmt = [0] * 8
        o1_toks = [0] * 8

        def d2_mms(mg, h=h, o1scr_h=o1scr_h, h3=h3):
            par = mg % 2
            ob = mg % 2
            w = [(dv, o1_used[ob])] if o1_used[ob] else []
            if mg == 0 and h == 0:
                w.append((qs, hst_dma[NFFC - 1]))
            o1_toks[mg] = p.dma("sync", o1t[ob][:],
                                o1scr_h[:, mg * 4:(mg + 1) * 4, :], waits=w)
            for fp in range(6):
                fpsz = 16 if fp < 5 else 6
                wb = fp % 3
                w = [(pe, dtb_used[wb])] if dtb_used[wb] else []
                if mg == 0 and fp < 2 and h == 0:
                    w.append((qs, hst_dma[NFFC - 1]))
                    w.append((pe, d1_last_mm))
                nel = 128 * fpsz * 512
                off = (fp * 8 + mg) * (128 * 16 * 512)
                dt_tok = p.dma("gpsimd", dtb[wb][:, 0:fpsz * 512],
                               dwt.ap()[0, off: off + nel]
                               .rearrange("(p k) -> p k", p=128), waits=w)
                for fl in range(fpsz):
                    fc = fp * 16 + fl
                    for mc in range(4):
                        wts = []
                        if fl == 0 and mc == 0:
                            wts = [(qg, dt_tok), (qs, hld[min(fc // 8, 10)])]
                            if fp == 0:
                                wts.append((dv, d2_bank_free[mg % 2]))
                        elif fl % 8 == 0 and mc == 0:
                            wts = [(qs, hld[min(fc // 8, 10)])]
                        last = (fl == fpsz - 1 and mc == 3)
                        p.op("tensor", lambda t_, wb=wb, fl=fl, mc=mc, fc=fc, par=par: t_.matmul(
                            bankf(mc + 4 * par),
                            dtb[wb][:, fl * 512 + mc * 128: fl * 512 + mc * 128 + 128],
                            h3[:, fc, :],
                            start=(fc == 0), stop=(fc == NFFC - 1)),
                            waits=wts, inc=pe if last else None)
                dtb_used[wb] = pe.n
            d2_mmt[mg] = pe.n

        def d2_ev(mg, h=h):
            par = mg % 2
            ob = mg % 2
            for mc in range(4):
                m = mg * 4 + mc
                rb = mc % 2
                w = [(pe, d2_mmt[mg]), (qs, o1_toks[mg])]
                if res_used[rb]:
                    w.append((pe, res_used[rb]))
                if stt_used[rb]:
                    w.append((dv, stt_used[rb]))
                stt_tok = p.op("vector", lambda v, mc=mc, par=par, m=m, ob=ob, rb=rb: v.scalar_tensor_tensor(
                    res_f[rb][:], bankf(mc + 4 * par), ds_sb[:, m:m + 1],
                    o1t[ob][:, mc * 512:(mc + 1) * 512],
                    op0=ALU.mult, op1=ALU.add), waits=w, inc=dv)
                for tt in range(4):
                    tok = p.op("tensor", lambda t_, mc=mc, par=par, tt=tt, rb=rb: t_.transpose(
                        PS[:, (mc + 4 * par) * 512 + tt * 128: (mc + 4 * par) * 512 + (tt + 1) * 128],
                        res_f[rb][:, tt * 128:(tt + 1) * 128], idf[:]),
                        waits=[(dv, stt_tok)] if tt == 0 else [],
                        inc=pe if tt == 3 else None)
                res_used[rb] = tok
                w = [(pe, tok)]
                if mg == 0 and mc == 0:
                    w.append((qg, xres_tok))
                at_ = 0
                for tt in range(4):
                    at_ = p.op("vector", lambda v, mc=mc, par=par, tt=tt, m=m: v.tensor_add(
                        outb3[:, tt, m * 128:(m + 1) * 128],
                        outb3[:, tt, m * 128:(m + 1) * 128],
                        PS[:, (mc + 4 * par) * 512 + tt * 128: (mc + 4 * par) * 512 + (tt + 1) * 128]),
                        waits=w if tt == 0 else [], inc=dv if tt == 3 else None)
                add_tok[mg] = at_
                stt_used[rb] = at_
            o1_used[ob] = add_tok[mg]
            d2_bank_free[mg % 2] = add_tok[mg]

        for mg in range(8):
            d2_mms(mg)
            if mg >= 1:
                d2_ev(mg - 1)
        d2_ev(7)
        d2_mm_last = d2_mmt[7]
        out3 = out.ap().rearrange("(tt p) d -> tt p d", p=128)
        ostg_tok = 0
        for tt in range(4):
            w = [(dv, add_tok[7])] if tt == 0 else []
            if ostg_tok:
                w.append((qs, ostg_tok))
            cp_tok = p.op("vector", lambda v, tt=tt: v.tensor_copy(
                ostg[:], outb3[:, tt, :]), waits=w, inc=dv)
            ostg_tok = p.dma("sync", out3[h * 4 + tt], ostg[:],
                             waits=[(dv, cp_tok)])
        out_store_tok = ostg_tok
        out_cp_last = cp_tok

    p.waitop("sync", [(qs, qs.n), (qg, qg.n)])
    p.waitop("gpsimd", [(qg, qg.n)])

    p.finalize()
    stack.close()
    return nc


def _prep_inputs(L0_attn_x, L0_attn_wq, L0_attn_wk, L0_attn_wv, L0_attn_wo,
                 L0_ffn_gate_w, L0_ffn_gate_s, L0_ffn_up_w, L0_ffn_up_s,
                 L0_ffn_down_w, L0_ffn_down_s):
    x = np.ascontiguousarray(np.asarray(L0_attn_x, dtype=np.float32).reshape(8192, DM))
    wq = np.asarray(L0_attn_wq, dtype=np.float32)
    wk = np.ascontiguousarray(np.asarray(L0_attn_wk, dtype=np.float32))
    wv = np.ascontiguousarray(np.asarray(L0_attn_wv, dtype=np.float32))
    wo = np.asarray(L0_attn_wo, dtype=np.float32)
    wqd = np.ascontiguousarray(
        wq.reshape(8, 4, 128, 16, 256).transpose(3, 0, 2, 1, 4)).reshape(-1)
    wod = np.ascontiguousarray(
        wo.reshape(8, 4, 128, 8, 512).transpose(3, 0, 2, 1, 4)).reshape(-1)
    gtT = np.asarray(L0_ffn_gate_w).astype(np.int8).T
    utT = np.asarray(L0_ffn_up_w).astype(np.int8).T
    dwtT = np.asarray(L0_ffn_down_w).astype(np.int8).T
    D1B_ = 128 * 16 * 512

    def ffn_blocks(wT):
        out_ = np.zeros(44 * D1B_, dtype=np.int8)
        G5 = wT.reshape(2, 16, 128, NFFC, 128)
        full = np.ascontiguousarray(
            G5[:, :, :, :84, :].reshape(2, 16, 128, 21, 4, 128)
            .transpose(3, 0, 2, 1, 4, 5))
        out_[: 42 * D1B_] = full.reshape(-1)
        last = np.ascontiguousarray(
            G5[:, :, :, 84:86, :].transpose(0, 2, 1, 3, 4))
        for dh in range(2):
            blk = last[dh].reshape(128, 16 * 256)
            dst = out_[(42 + dh) * D1B_: (42 + dh) * D1B_ + 128 * 16 * 256]
            dst.reshape(128, 16 * 256)[:] = blk
        return out_

    gt = ffn_blocks(gtT)
    ut = ffn_blocks(utT)
    dwt = np.zeros(48 * D1B_, dtype=np.int8)
    D5 = dwtT.reshape(NFFC, 128, 8, 512)
    fullD = np.ascontiguousarray(
        D5[:80].reshape(5, 16, 128, 8, 512).transpose(0, 3, 2, 1, 4))
    dwt[: 40 * D1B_] = fullD.reshape(-1)
    lastD = np.ascontiguousarray(D5[80:86].transpose(2, 1, 0, 3))
    for mg in range(8):
        blk = lastD[mg].reshape(128, 6 * 512)
        dst = dwt[(40 + mg) * D1B_: (40 + mg) * D1B_ + 128 * 6 * 512]
        dst.reshape(128, 6 * 512)[:] = blk
    gt = gt.reshape(1, -1); ut = ut.reshape(1, -1); dwt = dwt.reshape(1, -1)
    wqd = wqd.reshape(1, -1); wod = wod.reshape(1, -1)
    gs = np.ascontiguousarray(np.asarray(L0_ffn_gate_s, dtype=np.float32)
                              .reshape(NFFC, 128).T)
    us = np.ascontiguousarray(np.asarray(L0_ffn_up_s, dtype=np.float32)
                              .reshape(NFFC, 128).T)
    ds = np.ascontiguousarray(np.asarray(L0_ffn_down_s, dtype=np.float32)
                              .reshape(NDMC, 128).T)

    inv = (10000.0 ** (-np.arange(64, dtype=np.float64) / 64.0))
    maps = []
    for c in range(NCORE):
        xloc = np.zeros((TLOC, DM), dtype=np.float32)
        lo = (c - 1) * 1024
        if lo < 0:
            xloc[1024:] = x[0:1024]
        else:
            xloc[:] = x[lo: lo + TLOC]
        pos = np.maximum(lo + np.arange(TLOC), 0).astype(np.float64)
        ang = pos[None, :] * inv[:, None]
        cs = np.empty((128, TLOC), dtype=np.float32)
        cs[0:64] = np.cos(ang)
        cs[64:128] = np.sin(ang)
        i_ = np.arange(128)[None, :, None]
        w_ = np.arange(1152)[None, None, :]
        qt_ = np.arange(8)[:, None, None]
        qglob = c * 1024 + qt_ * 128 + i_
        jglob = lo + qt_ * 128 + w_
        valid = (jglob >= 0) & (jglob <= qglob) & (jglob > qglob - WIN)
        mask = np.where(valid, 0.0, NEG).astype(ml_dtypes.bfloat16)
        maps.append({
            "x": xloc, "wqd": wqd, "wk": wk, "wv": wv, "wod": wod,
            "gt": gt, "ut": ut, "dwt": dwt, "gs": gs, "us": us, "ds": ds,
            "cs": np.ascontiguousarray(cs), "mask": np.ascontiguousarray(mask),
        })
    return maps


def kernel(**inputs):
    maps = _prep_inputs(**inputs)
    nc = build_program()
    res = run_bass_kernel_spmd(nc, maps, core_ids=list(range(NCORE)))
    outs = [r["out"] for r in res.results]
    return np.concatenate(outs, axis=0).reshape(1, 8192, DM)


# version 14
# speedup vs baseline: 1.0368x; 1.0368x over previous
import numpy as np
import ml_dtypes
import concourse.bass as bass
import concourse.mybir as mybir
from concourse.bass_utils import run_bass_kernel_spmd
from contextlib import ExitStack

F32 = mybir.dt.float32
BF16 = mybir.dt.bfloat16
I32 = mybir.dt.int32
I8 = mybir.dt.int8
U8 = mybir.dt.uint8
AF = mybir.ActivationFunctionType
ALU = mybir.AluOpType
AX = mybir.AxisListType

DM = 4096
DFF = 11008
NCORE = 8
TLOC = 2048
TQ = 1024
NQH = 32
NKVH = 8
HD = 128
WIN = 1024
SM = HD ** -0.5
NEG = -1.0e30
NDMC = DM // 128
NFFC = DFF // 128

ENGS = ("tensor", "vector", "scalar", "sync", "gpsimd")
SERIALIZE_TP = False


class Sem:
    __slots__ = ("h", "n", "name")

    def __init__(self, h, name):
        self.h = h
        self.n = 0
        self.name = name


class Prog:
    def __init__(self, nc, stack):
        self.nc = nc
        self.ops = {e: [] for e in ENGS}
        self.sems = {}
        for s in ("pe", "dv", "ac", "qs", "qg", "gp"):
            self.sems[s] = Sem(stack.enter_context(nc.semaphore(s)), s)
        self._guard = None
        self._guarded = set()

    def phase(self):
        self._guard = {k: s.n for k, s in self.sems.items() if s.n > 0}
        self._guarded = set()

    def op(self, e, f, waits=(), inc=None, inc_n=1):
        wl = [(s.h, v) for (s, v) in waits if v and v > 0]
        if self._guard is not None and e not in self._guarded:
            self._guarded.add(e)
            wl = [(self.sems[k].h, v) for k, v in self._guard.items()] + wl
        tok = None
        ih = None
        if inc is not None:
            inc.n += inc_n
            tok = inc.n
            ih = inc.h

        def run(eng, f=f, wl=wl, ih=ih, inc_n=inc_n):
            for h, v in wl:
                eng.wait_ge(h, v)
            ins = f(eng)
            if ih is not None:
                ins.then_inc(ih, inc_n)
        self.ops[e].append(run)
        return tok

    def dma(self, e, dst, src, waits=(), transpose=False):
        q = self.sems["qs" if e == "sync" else "qg"]
        if transpose:
            w = list(waits)
            if SERIALIZE_TP and getattr(self, "_last_tp", 0):
                w.append((self.sems["qs"], self._last_tp))
            tok = self.op(e, lambda eng: eng.dma_start(dst, src, transpose=True),
                          waits=w, inc=q, inc_n=16)
            self._last_tp = tok
            return tok
        return self.op(e, lambda eng: eng.dma_start(dst, src), waits=waits,
                       inc=q, inc_n=16)

    def waitop(self, e, waits):
        wl = [(s.h, v) for (s, v) in waits if v and v > 0]

        def run(eng, wl=wl):
            for h, v in wl:
                eng.wait_ge(h, v)
        self.ops[e].append(run)

    def finalize(self):
        with self.nc.Block() as blk:
            for e in ENGS:
                def mk(e):
                    def body(eng):
                        for f in self.ops[e]:
                            f(eng)
                    return body
                getattr(blk, e)(mk(e))


def build_program(taps=()):
    taps = set(taps)
    nc = bass.Bass("TRN2", target_bir_lowering=False)

    x_in = nc.dram_tensor("x", [TLOC, DM], F32, kind="ExternalInput")
    wk = nc.dram_tensor("wk", [DM, NKVH * HD], F32, kind="ExternalInput")
    wv = nc.dram_tensor("wv", [DM, NKVH * HD], F32, kind="ExternalInput")
    D1B_ = 128 * 16 * 512
    gt = nc.dram_tensor("gt", [1, 44 * D1B_], I8, kind="ExternalInput")
    ut = nc.dram_tensor("ut", [1, 44 * D1B_], I8, kind="ExternalInput")
    dwt = nc.dram_tensor("dwt", [1, 48 * D1B_], I8, kind="ExternalInput")
    wqd = nc.dram_tensor("wqd", [1, DM * NQH * HD], F32, kind="ExternalInput")
    wod = nc.dram_tensor("wod", [1, NQH * HD * DM], F32, kind="ExternalInput")
    gs_in = nc.dram_tensor("gs", [128, NFFC], F32, kind="ExternalInput")
    us_in = nc.dram_tensor("us", [128, NFFC], F32, kind="ExternalInput")
    ds_in = nc.dram_tensor("ds", [128, NDMC], F32, kind="ExternalInput")
    cs_in = nc.dram_tensor("cs", [128, TLOC], F32, kind="ExternalInput")
    mk_in = nc.dram_tensor("mask", [8, 128, 1152], BF16, kind="ExternalInput")
    out = nc.dram_tensor("out", [TQ, DM], F32, kind="ExternalOutput")

    xnscr = nc.dram_tensor(
        "xnscr", [128, 16 * DM], BF16,
        kind="ExternalOutput" if "xn" in taps else "Internal")
    o1scr = nc.dram_tensor(
        "o1scr", [2, 128, NDMC, 512], BF16,
        kind="ExternalOutput" if "o1" in taps else "Internal")
    hscr = nc.dram_tensor(
        "hscr", [128, NFFC * 1024], BF16,
        kind="ExternalOutput" if "h" in taps else "Internal")
    qscr = nc.dram_tensor(
        "qscr", [128, NQH * 1024], BF16,
        kind="ExternalOutput" if "q" in taps else "Internal")

    def mktap(name, shape, dtype=BF16):
        return (nc.dram_tensor(name, shape, dtype, kind="ExternalOutput")
                if name in taps else None)

    k_tap = mktap("k_tap", [128, NKVH * TLOC])
    v_tap = mktap("v_tap", [128, 16 * NKVH * HD])
    q_tap = mktap("q_tap", [2, 128, NQH * 512])
    cat_tap = mktap("cat_tap", [2, 128, NQH * 512])
    h_tap = mktap("h_tap", [2, 128, NFFC * 512])

    stack = ExitStack()
    A = nc.alloc_sbuf_tensor

    cs_sb = A("cs_sb", [128, TLOC], F32)
    gs_sb = A("gs_sb", [128, NFFC], F32)
    us_sb = A("us_sb", [128, NFFC], F32)
    ds_sb = A("ds_sb", [128, NDMC], F32)
    epsb = A("epsb", [128, 1], F32)
    smv = [A(f"smv{i}", [128, 1], F32) for i in range(2)]
    ssqv = [A(f"ssqv{i}", [128, 1], F32) for i in range(2)]
    meanv = [A(f"meanv{i}", [128, 1], F32) for i in range(2)]
    varv = [A(f"varv{i}", [128, 1], F32) for i in range(2)]
    stdv = [A(f"stdv{i}", [128, 1], F32) for i in range(2)]
    rstd = [A(f"rstd{i}", [128, 1], F32) for i in range(2)]
    nmrv = [A(f"nmrv{i}", [128, 1], F32) for i in range(2)]
    rt1 = A("rt1", [64, 512], F32)
    rt2 = A("rt2", [64, 512], F32)
    nmx = [A(f"nmx{i}", [128, 1], F32) for i in range(2)]
    sbias = [A(f"sbias{i}", [128, 1], F32) for i in range(2)]
    sums = [A(f"sums{i}", [128, 1], F32) for i in range(2)]
    rsum = [A(f"rsum{i}", [128, 1], F32) for i in range(2)]

    ident_f = nc.inline_tensor(np.eye(128, dtype=np.float32), name="identf")
    idf = A("idf", [128, 128], F32)

    ARENA = 188416
    arena = A("arena", [128, ARENA], U8)
    base = nc.lookup_mloc(arena).addr
    assert nc.sbuf_bytes_remaining >= 0

    def at(name, shape, dtype, off):
        nbytes = 1
        for s_ in shape[1:]:
            nbytes *= s_
        nbytes *= mybir.dt.size(dtype)
        assert off + nbytes <= ARENA, f"{name}: {off}+{nbytes} > {ARENA}"
        return nc.alloc_sbuf_tensor_at(name, shape, dtype, offset=base + off)

    xb0 = [at(f"xb0_{i}", [128, DM], F32, i * 16384) for i in range(2)]
    xt0 = [at(f"xt0_{i}", [128, DM], BF16, 32768 + i * 8192) for i in range(2)]
    xf0 = [at(f"xf0_{i}", [128, DM], BF16, 49152 + i * 8192) for i in range(2)]
    K_fm = at("K_fm", [128, NKVH * TLOC], BF16, 0)
    V_tm = at("V_tm", [128, 16 * NKVH * HD], BF16, 32768)
    PB = 65536
    wkb = at("wkb", [128, NDMC * 512], BF16, PB)
    wvb = at("wvb", [128, NDMC * 512], BF16, PB + 32768)
    xsl = [at(f"xsl{i}", [128, 2 * DM], BF16, PB + 65536 + i * 16384)
           for i in range(2)]
    xqFq = at("xqFq", [128, 8 * DM], BF16, PB)
    qwb = [at(f"qwb{i}", [128, 4 * 256], BF16, PB + 65536 + i * 2048)
           for i in range(4)]
    Qst = [at(f"Qst{i}", [128, 2 * 1024], BF16, PB + 73728 + i * 4096)
           for i in range(2)]
    Qh = at("Qh", [128, NQH * 512], BF16, PB + 32768)
    cat = at("cat", [128, NQH * 512], BF16, PB)
    mtile = [at(f"mtile{i}", [128, 1152], BF16, PB + 65536 + i * 2304)
             for i in range(2)]
    sc_sb = [at(f"sc_sb{i}", [128, 1152], F32, PB + 70144 + i * 4608)
             for i in range(2)]
    sc0_sb = [at(f"sc0_sb{i}", [128, 1152], F32, PB + 89088 + i * 4608)
              for i in range(2)]
    P_sb = [at(f"P_sb{i}", [128, 1152], BF16, PB + 79360 + i * 2304)
            for i in range(2)]
    PT_sb = [at(f"PT_sb{i}", [128, 1152], BF16, PB + 83968 + i * 2304)
             for i in range(2)]
    av_sb = [at(f"av_sb{i}", [128, 128], BF16, PB + 88576 + i * 256)
             for i in range(2)]
    wob = [at(f"wob{i}", [128, 4 * 512], BF16, PB + 103424 + i * 4096)
           for i in range(4)]
    o1b = at("o1b", [128, 8 * 512], BF16, PB + 98304)
    xqF = at("xqF", [128, 8 * DM], BF16, 0)
    gtb = [at(f"gtb{i}", [128, 16 * 512], BF16, 65536 + i * 16384)
           for i in range(3)]
    utb = [at(f"utb{i}", [128, 16 * 512], BF16, 114688 + i * 16384)
           for i in range(3)]
    sgb = [at(f"sgb{i}", [128, 1024], BF16, 163840 + i * 2048) for i in range(2)]
    sub = [at(f"sub{i}", [128, 1024], BF16, 167936 + i * 2048) for i in range(2)]
    hst = [at(f"hst{i}", [128, 1024], BF16, 172032 + i * 2048) for i in range(2)]
    h_sb = at("h_sb", [128, NFFC * 512], BF16, 0)
    dtb = [at(f"dtb{i}", [128, 16 * 512], BF16, 90112 + i * 16384)
           for i in range(2)]
    dtb.append(at("dtb2", [128, 16 * 512], BF16, 167936))
    ostg = at("ostg", [128, DM], F32, 0)
    o1t = [at(f"o1t{i}", [128, 4 * 512], BF16, 122880 + i * 4096)
           for i in range(2)]
    res_f = [at(f"res_f{i}", [128, 512], F32, 131072 + i * 2048)
             for i in range(2)]
    outb = at("outb", [128, 4 * DM], BF16, 135168)

    PS = nc.alloc_psum_tensor("PS", [128, 4096], F32)

    def bankf(i, w=512):
        return PS[:, i * 512: i * 512 + w]

    p = Prog(nc, stack)
    pe, dv, ac, qs, qg, gp = (p.sems[k] for k in ("pe", "dv", "ac", "qs", "qg", "gp"))

    p.dma("sync", cs_sb[:], cs_in.ap())
    p.dma("sync", gs_sb[:], gs_in.ap())
    p.dma("sync", us_sb[:], us_in.ap())
    p.dma("sync", ds_sb[:], ds_in.ap())
    p.dma("sync", idf[:], ident_f.ap())
    pre_qs = qs.n
    p.op("vector", lambda v: v.memset(epsb[:], 1e-5),
         waits=[(qs, pre_qs)], inc=dv)

    xnscr3 = xnscr.ap().rearrange("p (tt d) -> p tt d", d=DM)

    p.phase()
    x2 = x_in.ap().rearrange("(tt p) d -> tt p d", p=128)
    ld_tok = [0] * 17
    ssq_tok = [0] * 17
    red_tok = [0] * 17
    xt_tok = [0] * 17
    dma_xn = [0] * 17
    ev_last = [0] * 17
    tp_last = [0] * 17
    tpb_ev = [0, 0]

    def p0_front(tt, pv=None):
        b = tt % 2
        w = []
        if pv is not None:
            w += [(dv, xt_tok[pv]), (ac, ssq_tok[pv])]
        ld_tok[tt] = p.dma("sync", xb0[b][:], x2[tt], waits=w)
        w = [(qs, ld_tok[tt])]
        if pv is not None:
            w.append((qs, ev_last[pv]))
        ssq_tok[tt] = p.op("scalar", lambda s, b=b: s.activation(
            xt0[b][:], xb0[b][:], AF.Square, accum_out=ssqv[b][:]),
            waits=w, inc=ac)
        red_tok[tt] = p.op("vector", lambda v, b=b: v.reduce_sum(
            smv[b][:], xb0[b][:], axis=AX.X),
            waits=[(qs, ld_tok[tt])], inc=dv)

    def p0_back(tt, pv=None):
        b = tt % 2
        t = p.op("vector", lambda v, b=b: v.tensor_scalar_mul(
            meanv[b][:], smv[b][:], 1.0 / DM), waits=[(dv, red_tok[tt])], inc=dv)
        t = p.op("vector", lambda v, b=b: v.tensor_scalar_mul(
            varv[b][:], ssqv[b][:], 1.0 / DM),
            waits=[(ac, ssq_tok[tt]), (dv, t)], inc=dv)
        t = p.op("vector", lambda v, b=b: v.tensor_mul(
            stdv[b][:], meanv[b][:], meanv[b][:]), waits=[(dv, t)], inc=dv)
        t = p.op("vector", lambda v, b=b: v.tensor_sub(
            varv[b][:], varv[b][:], stdv[b][:]), waits=[(dv, t)], inc=dv)
        t_std = p.op("scalar", lambda s, b=b: s.activation(
            stdv[b][:], varv[b][:], AF.Sqrt, bias=epsb[:]),
            waits=[(dv, t)], inc=ac)
        t = p.op("vector", lambda v, b=b: v.reciprocal(
            rstd[b][:], stdv[b][:]), waits=[(ac, t_std)], inc=dv)
        t = p.op("vector", lambda v, b=b: v.scalar_tensor_tensor(
            nmrv[b][:], meanv[b][:], -1.0, rstd[b][:],
            op0=ALU.mult, op1=ALU.mult), waits=[(dv, t)], inc=dv)
        w = [(dv, t), (ac, ssq_tok[tt])]
        if pv is not None:
            w.append((qs, ev_last[pv]))
        xt_tok[tt] = p.op("vector", lambda v, b=b: v.tensor_scalar(
            xt0[b][:], xb0[b][:], rstd[b][:], nmrv[b][:],
            op0=ALU.mult, op1=ALU.add), waits=w, inc=dv)
        w = [(dv, xt_tok[tt])]
        if pv is not None:
            w.append((qs, dma_xn[pv]))
        ev_last[tt] = p.dma(
            "sync",
            xf0[b].ap().rearrange("p (c t) -> p c t", t=128),
            xt0[b].ap().rearrange("p (c t) -> p c t", t=128),
            waits=w, transpose=True)
        tp_last[tt] = 0
        dma_xn[tt] = p.dma("sync", xnscr3[:, tt, :], xf0[b][:],
                           waits=[(qs, ev_last[tt])])

    P0ORD = list(range(8, 16)) + list(range(0, 8))
    for i_, tt in enumerate(P0ORD):
        p0_front(tt, P0ORD[i_ - 2] if i_ >= 2 else None)
        if i_ >= 1:
            p0_back(P0ORD[i_ - 1], P0ORD[i_ - 3] if i_ >= 3 else None)
    p0_back(P0ORD[15], P0ORD[13])

    qtiles_done = dma_xn[15]
    WQB = 128 * 4 * 256
    xqF_tok0 = p.dma("sync", xqFq[:], xnscr.ap()[:, 8 * DM:16 * DM],
                     waits=[(qs, qtiles_done)])
    xqF3g = xqFq.ap().rearrange("p (tt d) -> p tt d", d=DM)
    qw_used = [0, 0, 0, 0]
    qrope = [0] * 16
    qst_dma = [0] * 16
    qmm = [0] * 16

    def qf_mms(ag):
        par = ag % 2
        for dq in range(8):
            wb = (ag * 8 + dq) % 4
            w = [(pe, qw_used[wb])] if qw_used[wb] else []
            off = (ag * 8 + dq) * WQB
            wq_tok = p.dma("gpsimd", qwb[wb][:],
                           wqd.ap()[0, off: off + WQB]
                           .rearrange("(p k) -> p k", p=128), waits=w)
            for cl in range(4):
                dm = dq * 4 + cl
                for ach in range(2):
                    for th in range(2):
                        wts = []
                        if cl == 0 and ach == 0 and th == 0:
                            wts = [(qg, wq_tok)]
                            if dq == 0:
                                wts.append((qs, xqF_tok0))
                                if ag >= 2:
                                    wts.append((dv, qrope[ag - 2]))
                        bk = ach * 2 + th + 4 * par
                        p.op("tensor", lambda t_, wb=wb, cl=cl, ach=ach, th=th, dm=dm, bk=bk: t_.matmul(
                            bankf(bk),
                            qwb[wb][:, cl * 256 + ach * 128: cl * 256 + ach * 128 + 128],
                            xqF3g[:, th * 4:(th + 1) * 4, dm * 128:(dm + 1) * 128],
                            start=(dm == 0), stop=(dm == NDMC - 1)),
                            waits=wts,
                            inc=pe if (cl == 3 and ach == 1 and th == 1) else None)
            qw_used[wb] = pe.n
        qmm[ag] = pe.n

    def qf_ev(ag):
        par = ag % 2
        first = True
        for ach in range(2):
            for th in range(2):
                bk = ach * 2 + th + 4 * par
                pk = bankf(bk)
                cosw = cs_sb[0:64, 1024 + th * 512: 1024 + (th + 1) * 512]
                sinw = cs_sb[64:128, 1024 + th * 512: 1024 + (th + 1) * 512]
                d0 = Qst[par][0:64, ach * 1024 + th * 512: ach * 1024 + (th + 1) * 512]
                d1 = Qst[par][64:128, ach * 1024 + th * 512: ach * 1024 + (th + 1) * 512]
                w0 = []
                if first:
                    w0 = [(pe, qmm[ag])]
                    if ag >= 2:
                        w0.append((qs, qst_dma[ag - 2]))
                    first = False
                p.op("vector", lambda v, pk=pk, c=cosw: v.tensor_mul(rt1[:], pk[0:64, :], c), waits=w0)
                p.op("vector", lambda v, pk=pk, s=sinw: v.tensor_mul(rt2[:], pk[64:128, :], s))
                p.op("vector", lambda v, d=d0: v.tensor_sub(d, rt1[:], rt2[:]))
                p.op("vector", lambda v, pk=pk, s=sinw: v.tensor_mul(rt1[:], pk[0:64, :], s))
                p.op("vector", lambda v, pk=pk, c=cosw: v.tensor_mul(rt2[:], pk[64:128, :], c))
                qrope[ag] = p.op("vector", lambda v, d=d1: v.tensor_add(d, rt1[:], rt2[:]), inc=dv)
        qst_dma[ag] = p.dma("sync", qscr.ap()[:, ag * 2048:(ag + 1) * 2048],
                            Qst[par][:], waits=[(dv, qrope[ag])])

    for ag in range(16):
        qf_mms(ag)
        if ag >= 1:
            qf_ev(ag - 1)
    qf_ev(15)

    p.phase()
    wk3 = wk.ap().rearrange("(c p) f -> p c f", p=128)
    wv3 = wv.ap().rearrange("(c p) f -> p c f", p=128)
    xsl3 = [xsl[i].ap().rearrange("p (tt d) -> p tt d", d=DM) for i in range(2)]
    NKVC = 16
    kev = [0] * NKVC
    vev = [0] * NKVC
    mmt = [0] * NKVC
    xs_tok = [0] * NKVC
    wvb_toks = [0, 0]

    def kv_mm(it):
        fh, tch = it // 8, it % 8
        b = it % 2
        par = it % 2
        if tch == 0:
            w = [(pe, mmt[it - 1])] if it >= 1 else []
            p.dma("gpsimd", wkb[:], wk3[:, :, fh * 512:(fh + 1) * 512], waits=w)
            wvb_toks[fh] = p.dma("gpsimd", wvb[:],
                                 wv3[:, :, fh * 512:(fh + 1) * 512])
        w = [(pe, mmt[it - 2])] if it >= 2 else []
        xs_tok[it] = p.dma("sync", xsl[b][:],
                           xnscr.ap()[:, tch * 2 * DM:(tch + 1) * 2 * DM],
                           waits=w)
        for dm in range(NDMC):
            wts = []
            if dm == 0:
                wts = [(qs, xs_tok[it]), (qg, wvb_toks[fh])]
                if it >= 2:
                    wts += [(dv, kev[it - 2]), (ac, vev[it - 2])]
            for fc in range(4):
                bk = (fc // 2) + 4 * par
                sof = (fc % 2) * 256
                p.op("tensor", lambda t_, dm=dm, fc=fc, bk=bk, sof=sof, b=b: t_.matmul(
                    PS[:, bk * 512 + sof: bk * 512 + sof + 256],
                    wkb[:, dm * 512 + fc * 128: dm * 512 + fc * 128 + 128],
                    xsl3[b][:, :, dm * 128:(dm + 1) * 128],
                    start=(dm == 0), stop=(dm == NDMC - 1)),
                    waits=wts if fc == 0 else [])
            for ttl in range(2):
                last = (dm == NDMC - 1 and ttl == 1)
                p.op("tensor", lambda t_, dm=dm, ttl=ttl, par=par, b=b: t_.matmul(
                    bankf(2 + ttl + 4 * par),
                    xsl3[b][:, ttl, dm * 128:(dm + 1) * 128],
                    wvb[:, dm * 512:(dm + 1) * 512],
                    start=(dm == 0), stop=(dm == NDMC - 1)),
                    inc=pe if last else None)
        mmt[it] = pe.n

    def kv_ev(it):
        fh, tch = it // 8, it % 8
        par = it % 2
        for fc in range(4):
            kh = fh * 4 + fc
            bk = (fc // 2) + 4 * par
            sof = (fc % 2) * 256
            pk = PS[:, bk * 512 + sof: bk * 512 + sof + 256]
            cosw = cs_sb[0:64, tch * 256:(tch + 1) * 256]
            sinw = cs_sb[64:128, tch * 256:(tch + 1) * 256]
            d0 = K_fm[0:64, kh * TLOC + tch * 256: kh * TLOC + (tch + 1) * 256]
            d1 = K_fm[64:128, kh * TLOC + tch * 256: kh * TLOC + (tch + 1) * 256]
            w0 = [(pe, mmt[it])] if fc == 0 else []
            p.op("vector", lambda v, pk=pk, c=cosw: v.tensor_mul(rt1[:, 0:256], pk[0:64, :], c), waits=w0)
            p.op("vector", lambda v, pk=pk, s=sinw: v.tensor_mul(rt2[:, 0:256], pk[64:128, :], s))
            p.op("vector", lambda v, d=d0: v.tensor_sub(d, rt1[:, 0:256], rt2[:, 0:256]))
            p.op("vector", lambda v, pk=pk, s=sinw: v.tensor_mul(rt1[:, 0:256], pk[0:64, :], s))
            p.op("vector", lambda v, pk=pk, c=cosw: v.tensor_mul(rt2[:, 0:256], pk[64:128, :], c))
            kev[it] = p.op("vector", lambda v, d=d1: v.tensor_add(d, rt1[:, 0:256], rt2[:, 0:256]), inc=dv)
        for ttl in range(2):
            tc = tch * 2 + ttl
            fh_ = fh
            vev[it] = p.op("scalar", lambda s, tc=tc, ttl=ttl, par=par, fh=fh_: s.copy(
                V_tm[:, tc * 1024 + fh * 512: tc * 1024 + (fh + 1) * 512],
                bankf(2 + ttl + 4 * par)),
                waits=[(pe, mmt[it])] if ttl == 0 else [], inc=ac)

    for it in range(NKVC):
        kv_mm(it)
        if it >= 1:
            kv_ev(it - 1)
    kv_ev(NKVC - 1)
    if k_tap is not None:
        p.dma("sync", k_tap.ap(), K_fm[:], waits=[(dv, kev[NKVC - 1])])
    if v_tap is not None:
        p.dma("sync", v_tap.ap(), V_tm[:], waits=[(ac, vev[NKVC - 1])])

    o1_dma_tok = 0
    p3b_lastA = 0
    qscr3 = qscr.ap().rearrange("p (hd t) -> p hd t", t=1024)
    for h in range(2):
        p.phase()
        qh_tok = p.dma("sync", Qh[:], qscr3[:, :, h * 512:(h + 1) * 512],
                       waits=[(pe, p3b_lastA)] if p3b_lastA else [])

        p.phase()
        S = [PS[:, 0:1152], PS[:, 1536:2688]]
        Ssl = [(PS[:, 0:512], PS[:, 512:1024], PS[:, 1024:1152]),
               (PS[:, 1536:2048], PS[:, 2048:2560], PS[:, 2560:2688])]
        O_ps = [PS[:, 3072:3200], PS[:, 3584:3712]]
        mk3 = mk_in.ap()
        NH = 128
        m_tok = [0, 0]
        m_read = [0, 0]

        def hqt(i):
            return i // 32, i % 32

        tokA = [0] * NH
        tokE0 = [0] * NH
        tokB = [0] * NH
        tokB2 = [0] * NH
        tokC = [0] * NH
        tokG = [0] * NH
        tokT = [0] * NH
        tokF = [0] * NH
        tokN = [0] * NH
        tokH = [0] * NH

        def stA(i):
            qt, hd_ = hqt(i)
            qg_ = h * 4 + qt
            par = i % 2
            mb = qt % 2
            kh = hd_ // 4
            if hd_ == 0:
                w = [(qg, m_tok[mb])] if m_tok[mb] else []
                w += [(dv, m_read[mb])] if m_read[mb] else []
                m_tok[mb] = p.dma("gpsimd", mtile[mb][:], mk3[qg_], waits=w)
            wts = [(qg, m_tok[mb])] if hd_ == 0 else []
            if i == 0:
                wts.append((qs, qh_tok))
            if i >= 2:
                wts.append((dv, tokB[i - 2]))
            for j, wdt in enumerate((512, 512, 128)):
                off = qg_ * 128 + j * 512
                p.op("tensor", lambda t_, par=par, j=j, hd_=hd_, qt=qt, kh=kh, off=off, wdt=wdt: t_.matmul(
                    Ssl[par][j],
                    Qh[:, hd_ * 512 + qt * 128: hd_ * 512 + qt * 128 + 128],
                    K_fm[:, kh * TLOC + off: kh * TLOC + off + wdt],
                    start=True, stop=True),
                    waits=wts if j == 0 else [],
                    inc=pe if j == 2 else None)
            tokA[i] = pe.n

        def stBC(i):
            qt, hd_ = hqt(i)
            par = i % 2
            mb = qt % 2
            w = [(pe, tokA[i]), (qg, m_tok[mb])]
            if i >= 2:
                w.append((ac, tokC[i - 2]))
            tokB[i] = p.op("vector", lambda v, par=par, mb=mb: v.tensor_add(
                sc_sb[par][:], S[par], mtile[mb][:]), waits=w, inc=dv)
            m_read[mb] = tokB[i]
            t = p.op("vector", lambda v, par=par: v.reduce_max(
                nmx[par][:], sc_sb[par][:], axis=AX.X, negate=True),
                waits=[], inc=dv)
            w = [(dv, t)]
            if i >= 2:
                w.append((ac, tokC[i - 2]))
            tokB2[i] = p.op("vector", lambda v, par=par: v.tensor_scalar_mul(
                sbias[par][:], nmx[par][:], SM), waits=w, inc=dv)
            w = [(dv, tokB2[i])]
            if i >= 2:
                w += [(qs, tokT[i - 2]), (dv, tokG[i - 2])]
            tokC[i] = p.op("scalar", lambda s, par=par: s.activation(
                P_sb[par][:], sc_sb[par][:], AF.Exp, bias=sbias[par][:], scale=SM,
                accum_out=sums[par][:]), waits=w, inc=ac)
            tokG[i] = p.op("vector", lambda v, par=par: v.reciprocal(
                rsum[par][:], sums[par][:]), waits=[(ac, tokC[i])], inc=dv)
            w = [(ac, tokC[i])]
            if i >= 2:
                w.append((pe, tokF[i - 2]))
            tokT[i] = p.dma(
                "sync",
                PT_sb[par].ap().rearrange("p (c t) -> p c t", t=128),
                P_sb[par].ap().rearrange("p (c t) -> p c t", t=128),
                waits=w, transpose=True)

        def stFN(i):
            qt, hd_ = hqt(i)
            qg_ = h * 4 + qt
            par = i % 2
            kh = hd_ // 4
            for j in range(9):
                w = []
                if j == 0:
                    w = [(qs, tokT[i])]
                    if i >= 2:
                        w.append((ac, tokN[i - 2]))
                p.op("tensor", lambda t_, par=par, j=j, kh=kh, qg_=qg_: t_.matmul(
                    O_ps[par],
                    PT_sb[par][:, j * 128:(j + 1) * 128],
                    V_tm[:, (qg_ + j) * 1024 + kh * 128: (qg_ + j) * 1024 + kh * 128 + 128],
                    start=(j == 0), stop=(j == 8)),
                    waits=w, inc=pe if j == 8 else None)
            tokF[i] = pe.n
            w = [(pe, tokF[i]), (dv, tokG[i])]
            if i >= 2:
                w.append((qs, tokH[i - 2]))
            tokN[i] = p.op("scalar", lambda s, par=par: s.activation(
                av_sb[par][:], O_ps[par], AF.Copy, scale=rsum[par][:]),
                waits=w, inc=ac)
            tokH[i] = p.dma("sync",
                            cat[:, hd_ * 512 + qt * 128: hd_ * 512 + qt * 128 + 128],
                            av_sb[par][:], waits=[(ac, tokN[i])], transpose=True)

        for i in range(NH + 2):
            if i < NH:
                stA(i)
            if 1 <= i <= NH:
                stBC(i - 1)
            if i >= 2:
                stFN(i - 2)
        p3b_lastA = tokA[NH - 1]
        if cat_tap is not None:
            p.dma("sync", cat_tap.ap()[h], cat[:], waits=[(qs, tokH[NH - 1])])

        p.phase()
        o1scr_h = o1scr.ap()[h]
        o1b3 = o1b.ap().rearrange("p (dc t) -> p dc t", t=512)
        wo_used = [0, 0, 0, 0]
        wo_mm = [0] * 8
        wo_dma = [0] * 8

        def wo_mms(dg):
            par = dg % 2
            for aq in range(8):
                wb = (dg * 8 + aq) % 4
                w = [(pe, wo_used[wb])] if wo_used[wb] else []
                off = (dg * 8 + aq) * (128 * 4 * 512)
                wo_tok = p.dma("gpsimd", wob[wb][:],
                               wod.ap()[0, off: off + 128 * 4 * 512]
                               .rearrange("(p k) -> p k", p=128), waits=w)
                for cl in range(4):
                    a = aq * 4 + cl
                    for dc in range(4):
                        wts = []
                        if cl == 0 and dc == 0:
                            wts = [(qg, wo_tok)]
                            if aq == 0:
                                wts.append((qs, tokH[NH - 1]))
                                if dg >= 2:
                                    wts.append((qs, wo_dma[dg - 2]))
                                elif h > 0:
                                    wts.append((qs, o1_dma_tok))
                        last = (cl == 3 and dc == 3)
                        p.op("tensor", lambda t_, wb=wb, cl=cl, dc=dc, a=a, par=par: t_.matmul(
                            bankf(dc + 4 * par),
                            wob[wb][:, cl * 512 + dc * 128: cl * 512 + dc * 128 + 128],
                            cat[:, a * 512:(a + 1) * 512],
                            start=(a == 0), stop=(a == 31)),
                            waits=wts, inc=pe if last else None)
                wo_used[wb] = pe.n
            wo_mm[dg] = pe.n

        def wo_ev(dg):
            par = dg % 2
            ev = 0
            for dc in range(4):
                w = [(pe, wo_mm[dg])] if dc == 0 else []
                if dc == 0 and dg >= 2:
                    w.append((qs, wo_dma[dg - 2]))
                ev = p.op("scalar", lambda s, dc=dc, par=par: s.copy(
                    o1b3[:, 4 * par + dc, :], bankf(dc + 4 * par)),
                    waits=w, inc=ac if dc == 3 else None)
            wo_dma[dg] = p.dma("sync", o1scr_h[:, dg * 4:(dg + 1) * 4, :],
                               o1b3[:, 4 * par:4 * par + 4, :], waits=[(ac, ev)])

        for dg in range(8):
            wo_mms(dg)
            if dg >= 1:
                wo_ev(dg - 1)
        wo_ev(7)
        o1_dma_tok = wo_dma[7]
        p3c_cat_done = pe.n

    p.phase()
    xqF_tok = p.dma("sync", xqF[:], xnscr.ap()[:, 8 * DM:16 * DM])
    xqF3 = xqF.ap().rearrange("p (tt d) -> p tt d", d=DM)
    D1B = 128 * 16 * 512
    gtb_used = [0, 0, 0]
    utb_used = [0, 0, 0]
    d1_ut_tok = [0, 0]
    hst_dma = [0] * NFFC
    hm_tok = [0] * NFFC
    d1_mm = [0] * NFFC

    def d1_mms(fg):
        par = fg % 2
        fp = fg // 4
        fl4 = fg % 4
        for dh_ in range(2):
            wb = (fp * 2 + dh_) % 3
            if fl4 == 0:
                wide = min(4, NFFC - fg)
                nel = 128 * 16 * wide * 128
                off = (fp * 2 + dh_) * D1B
                w = [(pe, gtb_used[wb])] if gtb_used[wb] else []
                p.dma("gpsimd",
                      gtb[wb].ap().rearrange("p (a b) -> p a b", b=512)[:, :, 0:wide * 128],
                      gt.ap()[0, off: off + nel]
                      .rearrange("(p k) -> p k", p=128), waits=w)
                w = [(pe, utb_used[wb])] if utb_used[wb] else []
                d1_ut_tok[dh_] = p.dma(
                    "gpsimd",
                    utb[wb].ap().rearrange("p (a b) -> p a b", b=512)[:, :, 0:wide * 128],
                    ut.ap()[0, off: off + nel]
                    .rearrange("(p k) -> p k", p=128), waits=w)
            ut_tok = d1_ut_tok[dh_]
            for dml in range(16):
                dm = dh_ * 16 + dml
                wts = []
                if dml == 0:
                    wts = [(qg, ut_tok)]
                    if fg == 0 and dh_ == 0:
                        wts.append((qs, xqF_tok))
                    if dh_ == 0 and fg >= 2:
                        wts.append((dv, hm_tok[fg - 2]))
                for th in range(2):
                    p.op("tensor", lambda t_, wb=wb, dml=dml, dm=dm, th=th, par=par: t_.matmul(
                        bankf(th + 4 * par),
                        gtb[wb][:, dml * 512 + fl4 * 128: dml * 512 + fl4 * 128 + 128],
                        xqF3[:, th * 4:(th + 1) * 4, dm * 128:(dm + 1) * 128],
                        start=(dm == 0), stop=(dm == NDMC - 1)),
                        waits=wts if th == 0 else ())
                for th in range(2):
                    last = (dml == 15 and th == 1)
                    p.op("tensor", lambda t_, wb=wb, dml=dml, dm=dm, th=th, par=par: t_.matmul(
                        bankf(2 + th + 4 * par),
                        utb[wb][:, dml * 512 + fl4 * 128: dml * 512 + fl4 * 128 + 128],
                        xqF3[:, th * 4:(th + 1) * 4, dm * 128:(dm + 1) * 128],
                        start=(dm == 0), stop=(dm == NDMC - 1)),
                        inc=pe if last else None)
            if fl4 == 3 or fg == NFFC - 1:
                gtb_used[wb] = pe.n
                utb_used[wb] = pe.n
        d1_mm[fg] = pe.n

    def d1_ev(fg):
        par = fg % 2
        w0 = [(pe, d1_mm[fg])]
        if fg >= 2:
            w0.append((qs, hst_dma[fg - 2]))
        tmul = 0
        for th in range(2):
            t_sg = p.op("scalar", lambda s, th=th, par=par, fg=fg: s.activation(
                sgb[par][:, th * 512:(th + 1) * 512], bankf(th + 4 * par),
                AF.Silu, scale=gs_sb[:, fg:fg + 1]),
                waits=w0 if th == 0 else [], inc=ac)
            t_su = p.op("vector", lambda v, th=th, par=par, fg=fg: v.tensor_scalar_mul(
                sub[par][:, th * 512:(th + 1) * 512], bankf(2 + th + 4 * par),
                us_sb[:, fg:fg + 1]),
                waits=w0 if th == 0 else [], inc=dv)
            tmul = p.op("vector", lambda v, th=th, par=par: v.tensor_mul(
                hst[par][:, th * 512:(th + 1) * 512],
                sgb[par][:, th * 512:(th + 1) * 512],
                sub[par][:, th * 512:(th + 1) * 512]),
                waits=[(ac, t_sg), (dv, t_su)], inc=dv)
        hm_tok[fg] = tmul
        hst_dma[fg] = p.dma("sync", hscr.ap()[:, fg * 1024:(fg + 1) * 1024],
                            hst[par][:], waits=[(dv, tmul)])

    for fg in range(NFFC):
        d1_mms(fg)
        if fg >= 1:
            d1_ev(fg - 1)
    d1_ev(NFFC - 1)
    d1_last_mm = d1_mm[NFFC - 1]

    FOS = [8] * 10 + [6]
    d2_mm_last = 0
    dtb_used = [0, 0, 0]
    o1_used = [0, 0]
    res_used = [0, 0]
    stt_used = [0, 0]
    d2_bank_free = {0: hm_tok[NFFC - 2], 1: hm_tok[NFFC - 1]}
    out_store_tok = 0
    out_cp_last = 0
    for h in range(2):
        x2q = x_in.ap().rearrange("(tt p) d -> tt p d", p=128)
        outb3 = outb.ap().rearrange("p (tt d) -> p tt d", d=DM)
        xres_tok = 0
        for tt in range(4):
            w = []
            if tt == 0 and h == 0:
                w = [(pe, d1_last_mm), (qs, hst_dma[NFFC - 1])]
            if tt == 0 and h == 1:
                w = [(dv, out_cp_last)]
            xres_tok = p.dma("gpsimd", outb3[:, tt, :], x2q[8 + h * 4 + tt],
                             waits=w)
        o1scr_h = o1scr.ap()[h]
        h3 = h_sb.ap().rearrange("p (fc t) -> p fc t", t=512)
        hscr3 = hscr.ap().rearrange("p (fc t) -> p fc t", t=1024)
        hld = [0] * 11
        fo_base = 0
        for fo in range(11):
            fosz = FOS[fo]
            w = [(qs, hst_dma[fo_base + fosz - 1])]
            if h == 0 and fo == 0:
                w.append((pe, d1_last_mm))
            if h == 1 and fo == 0:
                w.append((pe, d2_mm_last))
                w.append((qs, out_store_tok))
            hld[fo] = p.dma("sync",
                            h3[:, fo * 8: fo * 8 + fosz, :],
                            hscr3[:, fo * 8: fo * 8 + fosz, h * 512:(h + 1) * 512],
                            waits=w)
            fo_base += fosz
        add_tok = [0] * 8
        d2_mmt = [0] * 8
        o1_toks = [0] * 8

        def d2_mms(mg, h=h, o1scr_h=o1scr_h, h3=h3):
            par = mg % 2
            ob = mg % 2
            w = [(dv, o1_used[ob])] if o1_used[ob] else []
            if mg == 0 and h == 0:
                w.append((qs, hst_dma[NFFC - 1]))
            o1_toks[mg] = p.dma("sync", o1t[ob][:],
                                o1scr_h[:, mg * 4:(mg + 1) * 4, :], waits=w)
            for fp in range(6):
                fpsz = 16 if fp < 5 else 6
                wb = fp % 3
                w = [(pe, dtb_used[wb])] if dtb_used[wb] else []
                if mg == 0 and fp < 3 and h == 0:
                    w.append((qs, hst_dma[NFFC - 1]))
                    w.append((pe, d1_last_mm))
                nel = 128 * fpsz * 512
                off = (fp * 8 + mg) * (128 * 16 * 512)
                dt_tok = p.dma("gpsimd", dtb[wb][:, 0:fpsz * 512],
                               dwt.ap()[0, off: off + nel]
                               .rearrange("(p k) -> p k", p=128), waits=w)
                for fl in range(fpsz):
                    fc = fp * 16 + fl
                    for mc in range(4):
                        wts = []
                        if fl == 0 and mc == 0:
                            wts = [(qg, dt_tok), (qs, hld[min(fc // 8, 10)])]
                            if fp == 0:
                                wts.append((dv, d2_bank_free[mg % 2]))
                        elif fl % 8 == 0 and mc == 0:
                            wts = [(qs, hld[min(fc // 8, 10)])]
                        last = (fl == fpsz - 1 and mc == 3)
                        p.op("tensor", lambda t_, wb=wb, fl=fl, mc=mc, fc=fc, par=par: t_.matmul(
                            bankf(mc + 4 * par),
                            dtb[wb][:, fl * 512 + mc * 128: fl * 512 + mc * 128 + 128],
                            h3[:, fc, :],
                            start=(fc == 0), stop=(fc == NFFC - 1)),
                            waits=wts, inc=pe if last else None)
                dtb_used[wb] = pe.n
            d2_mmt[mg] = pe.n

        def d2_ev(mg, h=h):
            par = mg % 2
            ob = mg % 2
            for mc in range(4):
                m = mg * 4 + mc
                rb = mc % 2
                w = [(pe, d2_mmt[mg]), (qs, o1_toks[mg])]
                if res_used[rb]:
                    w.append((pe, res_used[rb]))
                if stt_used[rb]:
                    w.append((dv, stt_used[rb]))
                stt_tok = p.op("vector", lambda v, mc=mc, par=par, m=m, ob=ob, rb=rb: v.scalar_tensor_tensor(
                    res_f[rb][:], bankf(mc + 4 * par), ds_sb[:, m:m + 1],
                    o1t[ob][:, mc * 512:(mc + 1) * 512],
                    op0=ALU.mult, op1=ALU.add), waits=w, inc=dv)
                for tt in range(4):
                    tok = p.op("tensor", lambda t_, mc=mc, par=par, tt=tt, rb=rb: t_.transpose(
                        PS[:, (mc + 4 * par) * 512 + tt * 128: (mc + 4 * par) * 512 + (tt + 1) * 128],
                        res_f[rb][:, tt * 128:(tt + 1) * 128], idf[:]),
                        waits=[(dv, stt_tok)] if tt == 0 else [],
                        inc=pe if tt == 3 else None)
                res_used[rb] = tok
                w = [(pe, tok)]
                if mg == 0 and mc == 0:
                    w.append((qg, xres_tok))
                at_ = 0
                for tt in range(4):
                    at_ = p.op("vector", lambda v, mc=mc, par=par, tt=tt, m=m: v.tensor_add(
                        outb3[:, tt, m * 128:(m + 1) * 128],
                        outb3[:, tt, m * 128:(m + 1) * 128],
                        PS[:, (mc + 4 * par) * 512 + tt * 128: (mc + 4 * par) * 512 + (tt + 1) * 128]),
                        waits=w if tt == 0 else [], inc=dv if tt == 3 else None)
                add_tok[mg] = at_
                stt_used[rb] = at_
            o1_used[ob] = add_tok[mg]
            d2_bank_free[mg % 2] = add_tok[mg]

        for mg in range(8):
            d2_mms(mg)
            if mg >= 1:
                d2_ev(mg - 1)
        d2_ev(7)
        d2_mm_last = d2_mmt[7]
        out3 = out.ap().rearrange("(tt p) d -> tt p d", p=128)
        ostg_tok = 0
        for tt in range(4):
            w = [(dv, add_tok[7])] if tt == 0 else []
            if ostg_tok:
                w.append((qs, ostg_tok))
            cp_tok = p.op("vector", lambda v, tt=tt: v.tensor_copy(
                ostg[:], outb3[:, tt, :]), waits=w, inc=dv)
            ostg_tok = p.dma("sync", out3[h * 4 + tt], ostg[:],
                             waits=[(dv, cp_tok)])
        out_store_tok = ostg_tok
        out_cp_last = cp_tok

    p.waitop("sync", [(qs, qs.n), (qg, qg.n)])
    p.waitop("gpsimd", [(qg, qg.n)])

    p.finalize()
    stack.close()
    return nc


def _prep_inputs(L0_attn_x, L0_attn_wq, L0_attn_wk, L0_attn_wv, L0_attn_wo,
                 L0_ffn_gate_w, L0_ffn_gate_s, L0_ffn_up_w, L0_ffn_up_s,
                 L0_ffn_down_w, L0_ffn_down_s):
    x = np.ascontiguousarray(np.asarray(L0_attn_x, dtype=np.float32).reshape(8192, DM))
    wq = np.asarray(L0_attn_wq, dtype=np.float32)
    wk = np.ascontiguousarray(np.asarray(L0_attn_wk, dtype=np.float32))
    wv = np.ascontiguousarray(np.asarray(L0_attn_wv, dtype=np.float32))
    wo = np.asarray(L0_attn_wo, dtype=np.float32)
    wqd = np.ascontiguousarray(
        wq.reshape(8, 4, 128, 16, 256).transpose(3, 0, 2, 1, 4)).reshape(-1)
    wod = np.ascontiguousarray(
        wo.reshape(8, 4, 128, 8, 512).transpose(3, 0, 2, 1, 4)).reshape(-1)
    gtT = np.asarray(L0_ffn_gate_w).astype(np.int8).T
    utT = np.asarray(L0_ffn_up_w).astype(np.int8).T
    dwtT = np.asarray(L0_ffn_down_w).astype(np.int8).T
    D1B_ = 128 * 16 * 512

    def ffn_blocks(wT):
        out_ = np.zeros(44 * D1B_, dtype=np.int8)
        G5 = wT.reshape(2, 16, 128, NFFC, 128)
        full = np.ascontiguousarray(
            G5[:, :, :, :84, :].reshape(2, 16, 128, 21, 4, 128)
            .transpose(3, 0, 2, 1, 4, 5))
        out_[: 42 * D1B_] = full.reshape(-1)
        last = np.ascontiguousarray(
            G5[:, :, :, 84:86, :].transpose(0, 2, 1, 3, 4))
        for dh in range(2):
            blk = last[dh].reshape(128, 16 * 256)
            dst = out_[(42 + dh) * D1B_: (42 + dh) * D1B_ + 128 * 16 * 256]
            dst.reshape(128, 16 * 256)[:] = blk
        return out_

    gt = ffn_blocks(gtT)
    ut = ffn_blocks(utT)
    dwt = np.zeros(48 * D1B_, dtype=np.int8)
    D5 = dwtT.reshape(NFFC, 128, 8, 512)
    fullD = np.ascontiguousarray(
        D5[:80].reshape(5, 16, 128, 8, 512).transpose(0, 3, 2, 1, 4))
    dwt[: 40 * D1B_] = fullD.reshape(-1)
    lastD = np.ascontiguousarray(D5[80:86].transpose(2, 1, 0, 3))
    for mg in range(8):
        blk = lastD[mg].reshape(128, 6 * 512)
        dst = dwt[(40 + mg) * D1B_: (40 + mg) * D1B_ + 128 * 6 * 512]
        dst.reshape(128, 6 * 512)[:] = blk
    gt = gt.reshape(1, -1); ut = ut.reshape(1, -1); dwt = dwt.reshape(1, -1)
    wqd = wqd.reshape(1, -1); wod = wod.reshape(1, -1)
    gs = np.ascontiguousarray(np.asarray(L0_ffn_gate_s, dtype=np.float32)
                              .reshape(NFFC, 128).T)
    us = np.ascontiguousarray(np.asarray(L0_ffn_up_s, dtype=np.float32)
                              .reshape(NFFC, 128).T)
    ds = np.ascontiguousarray(np.asarray(L0_ffn_down_s, dtype=np.float32)
                              .reshape(NDMC, 128).T)

    inv = (10000.0 ** (-np.arange(64, dtype=np.float64) / 64.0))
    maps = []
    for c in range(NCORE):
        xloc = np.zeros((TLOC, DM), dtype=np.float32)
        lo = (c - 1) * 1024
        if lo < 0:
            xloc[1024:] = x[0:1024]
        else:
            xloc[:] = x[lo: lo + TLOC]
        pos = np.maximum(lo + np.arange(TLOC), 0).astype(np.float64)
        ang = pos[None, :] * inv[:, None]
        cs = np.empty((128, TLOC), dtype=np.float32)
        cs[0:64] = np.cos(ang)
        cs[64:128] = np.sin(ang)
        i_ = np.arange(128)[None, :, None]
        w_ = np.arange(1152)[None, None, :]
        qt_ = np.arange(8)[:, None, None]
        qglob = c * 1024 + qt_ * 128 + i_
        jglob = lo + qt_ * 128 + w_
        valid = (jglob >= 0) & (jglob <= qglob) & (jglob > qglob - WIN)
        mask = np.where(valid, 0.0, NEG).astype(ml_dtypes.bfloat16)
        maps.append({
            "x": xloc, "wqd": wqd, "wk": wk, "wv": wv, "wod": wod,
            "gt": gt, "ut": ut, "dwt": dwt, "gs": gs, "us": us, "ds": ds,
            "cs": np.ascontiguousarray(cs), "mask": np.ascontiguousarray(mask),
        })
    return maps


def kernel(**inputs):
    maps = _prep_inputs(**inputs)
    nc = build_program()
    res = run_bass_kernel_spmd(nc, maps, core_ids=list(range(NCORE)))
    outs = [r["out"] for r in res.results]
    return np.concatenate(outs, axis=0).reshape(1, 8192, DM)
